# version 01
# baseline (speedup 1.0000x reference)
import contextlib
import numpy as np
import ml_dtypes
import concourse.bass as bass
import concourse.mybir as mybir
from concourse.bass_utils import run_bass_kernel_spmd

F32 = mybir.dt.float32
BF16 = mybir.dt.bfloat16
AF = mybir.ActivationFunctionType
ALU = mybir.AluOpType
AX = mybir.AxisListType

B, T, E, H, D = 4, 4096, 2048, 16, 128
HF = 64
FQ = 6144
I = 5632
IC = I // 128
EC = E // 128
TOWN = 2048
NSLOT = 16
NG = 4
NT128_KV = T // 128
NT128_OWN = TOWN // 128
C0 = 14.0
EPS = 1e-5
INV_E = 1.0 / 2048.0
BF = ml_dtypes.bfloat16

ENGS = ("tensor", "vector", "scalar", "sync", "gpsimd")


class Sem:
    def __init__(self, handle, name):
        self.h = handle
        self.name = name
        self.n = 0

    def inc(self, amt):
        self.n += amt
        return self.n


class Sched:

    def __init__(self):
        self.q = {e: [] for e in ENGS}

    def add(self, eng, fn):
        self.q[eng].append(fn)

    def dma(self, eng, sem, out, in_, transpose=False):
        v = sem.inc(16)

        def f(E, out=out, in_=in_, sem=sem, transpose=transpose):
            E.dma_start(out=out, in_=in_, transpose=transpose).then_inc(sem.h, 16)
        self.add(eng, f)
        return v

    def wait(self, eng, sem, val):
        assert val <= sem.n, f"wait on {sem.name} for {val} > planned {sem.n}"

        def f(E, sem=sem, val=val):
            E.wait_ge(sem.h, val)
        self.add(eng, f)

    def waitF(self, eng, sem):
        self.wait(eng, sem, sem.n)

    def op(self, eng, fn, sem=None):
        if sem is None:
            self.add(eng, lambda E, fn=fn: fn(E))
            return None
        v = sem.inc(1)

        def f(E, fn=fn, sem=sem):
            fn(E).then_inc(sem.h, 1)
        self.add(eng, f)
        return v

    def vnop(self):
        self.add("vector", lambda E: E.nop())

    def barrier(self, bar):
        target = bar.n + len(ENGS)
        for e in ENGS:
            def f(E, bar=bar):
                E.nop().then_inc(bar.h, 1)
            self.add(e, f)
        bar.n = target
        for e in ENGS:
            self.wait(e, bar, target)


def ts(i, w):
    return slice(i * w, (i + 1) * w)


def build_program(debug=False, nphases=6):
    nc = bass.Bass("TRN2", target_bir_lowering=False)

    def din(name, shape, dt=F32):
        return nc.dram_tensor(name, list(shape), dt, kind="ExternalInput").ap()

    x_kv = din("x_kv", (T, E))
    x_own = din("x_own", (TOWN, E))
    wqT = din("wqT", (E, E))
    wkT = din("wkT", (E, E))
    wvT = din("wvT", (E, E))
    wprojC = din("wprojC", (EC, E, 128))
    gateC = din("gateC", (IC, E, 128))
    upC = din("upC", (IC, E, 128))
    downC = din("downC", (EC, I, 128))
    gate_s = din("gate_s", (128, IC))
    up_s = din("up_s", (128, IC))
    down_s = din("down_s", (128, EC))
    qcos_hi = din("qcos_hi", (128, NSLOT * HF))
    qcos_lo = din("qcos_lo", (128, NSLOT * HF))
    qsin_hi = din("qsin_hi", (128, NSLOT * HF))
    qsin_lo = din("qsin_lo", (128, NSLOT * HF))
    kcos_hi = din("kcos_hi", (128, NT128_KV * HF))
    kcos_lo = din("kcos_lo", (128, NT128_KV * HF))
    ksin_hi = din("ksin_hi", (128, NT128_KV * HF))
    ksin_lo = din("ksin_lo", (128, NT128_KV * HF))
    pmask = din("pmask", (128, NG * 8 * 512), BF16)
    ident_bf = din("ident_bf", (128, 128), BF16)
    ones_col_bf = din("ones_col_bf", (128, 1), BF16)
    ones_row_f = din("ones_row_f", (1, 128))
    negc0 = din("negc0", (128, 1))

    out = nc.dram_tensor("out", [TOWN, E], F32, kind="ExternalOutput").ap()

    skind = "ExternalOutput" if debug else "Internal"
    ln1kvT = nc.dram_tensor("s_ln1kvT", [EC, 128, T], BF16, kind=skind).ap()
    ln1ownT = nc.dram_tensor("s_ln1ownT", [EC, 128, TOWN], BF16, kind=skind).ap()
    qtok = nc.dram_tensor("s_q", [TOWN, E], BF16, kind=skind).ap()
    ktok = nc.dram_tensor("s_k", [T, E], BF16, kind=skind).ap()
    qTp = nc.dram_tensor("s_qT", [H, D, TOWN], BF16, kind=skind).ap()
    kTp = nc.dram_tensor("s_kT", [H, D, T], BF16, kind=skind).ap()
    vsc = nc.dram_tensor("s_v", [H, T, D], BF16, kind=skind).ap()
    ysc = nc.dram_tensor("s_y", [H, D, TOWN], BF16, kind=skind).ap()
    x1sc = nc.dram_tensor("s_x1", [TOWN, E], F32, kind=skind).ap()
    ln2T = nc.dram_tensor("s_ln2T", [EC, 128, TOWN], BF16, kind=skind).ap()
    hsc = nc.dram_tensor("s_h", [IC, 128, TOWN], BF16, kind=skind).ap()

    sc = Sched()

    with contextlib.ExitStack() as stk:
        ecm = stk.enter_context

        def mksem(name):
            return Sem(ecm(nc.semaphore(name)), name)

        est, esv, ess, bar = mksem("est"), mksem("esv"), mksem("ess"), mksem("bar")
        sTab = mksem("sTab")

        ident_sb = ecm(nc.sbuf_tensor("ident_sb", [128, 128], BF16))
        onesc_sb = ecm(nc.sbuf_tensor("onesc_sb", [128, 1], BF16))
        onesr_sb = ecm(nc.sbuf_tensor("onesr_sb", [1, 128], F32))
        gs_sb = ecm(nc.sbuf_tensor("gs_sb", [128, IC], F32))
        us_sb = ecm(nc.sbuf_tensor("us_sb", [128, IC], F32))
        ds_sb = ecm(nc.sbuf_tensor("ds_sb", [128, EC], F32))
        negc0_sb = ecm(nc.sbuf_tensor("negc0_sb", [128, 1], F32))

        for dst, src in [(ident_sb, ident_bf), (onesc_sb, ones_col_bf),
                         (onesr_sb, ones_row_f), (gs_sb, gate_s),
                         (us_sb, up_s), (ds_sb, down_s), (negc0_sb, negc0)]:
            sc.dma("sync", sTab, dst[:], src[:])

        def ln_tile(xbuf, lnbuf, stats, dep_sem, dep_val):
            s1, ssq, m_, var_, rstd, rstd2, mb, sqs = (
                stats["s1"], stats["ssq"], stats["m"], stats["var"],
                stats["rstd"], stats["rstd2"], stats["mb"], stats["sqscr"])
            sc.wait("scalar", dep_sem, dep_val)
            v_sq = sc.op("scalar", lambda E: E.activation(
                out=sqs[:], in_=xbuf[:], func=AF.Square, accum_out=ssq[:]), ess)
            sc.wait("vector", dep_sem, dep_val)
            last = sc.op("vector", lambda E: E.reduce_sum(s1[:], xbuf[:], axis=AX.X), esv)

            def step(fn, extra=None):
                nonlocal last
                sc.wait("vector", esv, last)
                if extra is not None:
                    sc.wait("vector", extra[0], extra[1])
                last = sc.op("vector", fn, esv)
            step(lambda E: E.tensor_scalar_mul(m_[:], s1[:], INV_E), (ess, v_sq))
            step(lambda E: E.tensor_scalar_mul(var_[:], ssq[:], INV_E))
            step(lambda E: E.tensor_tensor(out=s1[:], in0=m_[:], in1=m_[:], op=ALU.mult))
            step(lambda E: E.tensor_tensor(out=var_[:], in0=var_[:], in1=s1[:], op=ALU.subtract))
            step(lambda E: E.tensor_scalar_add(var_[:], var_[:], EPS))
            v_var = last
            sc.wait("scalar", esv, v_var)
            v_sd = sc.op("scalar", lambda E: E.activation(
                out=rstd[:], in_=var_[:], func=AF.Sqrt), ess)
            step(lambda E: E.reciprocal(rstd2[:], rstd[:]), (ess, v_sd))
            step(lambda E: E.tensor_tensor(out=mb[:], in0=m_[:], in1=rstd2[:], op=ALU.mult))
            step(lambda E: E.tensor_scalar_mul(mb[:], mb[:], -1.0))
            v_mb = last
            sc.wait("scalar", esv, v_mb)
            v_ln = sc.op("scalar", lambda E: E.activation(
                out=lnbuf[:], in_=xbuf[:], func=AF.Identity, bias=mb[:], scale=rstd2[:]), ess)
            return v_ln

        def mkstats(e, pfx):
            st = {k: e(nc.sbuf_tensor(f"{pfx}_{k}", [128, 1], F32))
                  for k in ("s1", "ssq", "m", "var", "rstd", "rstd2", "mb")}
            st["sqscr"] = e(nc.sbuf_tensor(f"{pfx}_sqscr", [128, E], F32))
            return st

        def _phase0():
            with contextlib.ExitStack() as p0:
                e0 = p0.enter_context
                xb = [e0(nc.sbuf_tensor(f"p0x{i}", [128, E], F32)) for i in range(2)]
                lb = [e0(nc.sbuf_tensor(f"p0l{i}", [128, E], BF16)) for i in range(2)]
                tb_ = [e0(nc.sbuf_tensor(f"p0t{i}", [128, E], BF16)) for i in range(2)]
                stats = mkstats(e0, "p0")
                psT = [e0(nc.psum_tensor(f"p0psT{i}", [128, 512], BF16)) for i in range(2)]
                sX = [mksem("p0x0"), mksem("p0x1")]
                sSt = [mksem("p0s0"), mksem("p0s1")]

                tiles = [(ln1ownT, x_own, i) for i in range(NT128_OWN)] + \
                        [(ln1kvT, x_kv, i) for i in range(NT128_KV)]
                hist = []
                prev_ev = {}
                evs = []
                for idx, (dst, src, i) in enumerate(tiles):
                    bi = idx % 2
                    if idx >= 2:
                        sc.wait("sync", ess, hist[idx - 2])
                    sc.dma("sync", sX[bi], xb[bi][:], src[ts(i, 128), :])
                    if idx >= 2:
                        sc.wait("scalar", est, prev_ev[(idx - 2, "trlast")])
                    v_ln = ln_tile(xb[bi], lb[bi], stats, sX[bi], sX[bi].n)
                    sc.wait("tensor", ess, v_ln)
                    sc.waitF("tensor", sTab)
                    for q4 in range(4):
                        bk = psT[q4 % 2]
                        if len(evs) >= 2:
                            sc.wait("tensor", esv, evs[-2])
                        v_tr = 0
                        for j in range(4):
                            ec = q4 * 4 + j
                            v_tr = sc.op("tensor", lambda E, ec=ec, bk=bk, bi=bi, j=j: E.transpose(
                                bk[:, ts(j, 128)], lb[bi][:, ts(ec, 128)], ident_sb[:]), est)
                        if q4 == 3:
                            prev_ev[(idx, "trlast")] = v_tr
                        sc.wait("vector", est, v_tr)
                        if q4 == 0:
                            sc.waitF("vector", sSt[bi])
                        evs.append(sc.op("vector", lambda E, q4=q4, bk=bk, bi=bi: E.tensor_copy(
                            tb_[bi][:, ts(q4, 512)], bk[:]), esv))
                    sc.wait("gpsimd", esv, evs[-1])
                    for ec in range(EC):
                        sc.dma("gpsimd", sSt[bi], dst[ec, :, ts(i, 128)],
                               tb_[bi][:, ts(ec, 128)])
                    hist.append(v_ln)
                sc.waitF("gpsimd", sSt[0])
                sc.waitF("gpsimd", sSt[1])
                sc.barrier(bar)
        if nphases > 0:
            _phase0()

        def _phase1():
            with contextlib.ExitStack() as p1:
                e1 = p1.enter_context
                W_sb = e1(nc.sbuf_tensor("p1W", [128, EC * 2048], BF16))
                lnT = e1(nc.sbuf_tensor("p1lnT", [128, EC * 2048], BF16))
                qc = e1(nc.sbuf_tensor("p1qc", [128, NSLOT * HF], F32))
                qs_ = e1(nc.sbuf_tensor("p1qs", [128, NSLOT * HF], F32))
                kc_ = e1(nc.sbuf_tensor("p1kc", [128, NT128_KV * HF], F32))
                ks_ = e1(nc.sbuf_tensor("p1ks", [128, NT128_KV * HF], F32))
                tlo = e1(nc.sbuf_tensor("p1tlo", [128, NT128_KV * HF], F32))
                tmp = [e1(nc.sbuf_tensor(f"p1tmp{i}", [128, 256], F32)) for i in range(4)]
                evb = [e1(nc.sbuf_tensor(f"p1ev{i}", [128, 2048], BF16)) for i in range(2)]
                p1ps = contextlib.ExitStack()
                psP1 = [p1ps.enter_context(nc.psum_tensor(f"p1ps{i}", [128, 512], F32))
                        for i in range(8)]
                sW = mksem("p1w")
                sLnT = mksem("p1lnt")
                sSt = [mksem("p1s0"), mksem("p1s1")]

                v_tab = 0
                for wk, hi, lo in [(qc, qcos_hi, qcos_lo), (qs_, qsin_hi, qsin_lo),
                                   (kc_, kcos_hi, kcos_lo), (ks_, ksin_hi, ksin_lo)]:
                    n = wk.shape[1]
                    if v_tab:
                        sc.wait("sync", esv, v_tab)
                    sc.dma("sync", sTab, wk[:], hi[:])
                    sc.dma("sync", sTab, tlo[:, 0:n], lo[:])
                    sc.waitF("vector", sTab)
                    v_tab = sc.op("vector", lambda E, wk=wk, n=n: E.tensor_tensor(
                        out=wk[:], in0=wk[:], in1=tlo[:, 0:n], op=ALU.add), esv)

                def rope_evac(bank, evslice, ctab, stab, tile_idx, v_mm_done, v_dep):
                    cP = ctab[:, ts(tile_idx, HF)].rearrange(
                        "p (o j) -> p o j", o=1).broadcast_to((128, 4, HF))
                    sP = stab[:, ts(tile_idx, HF)].rearrange(
                        "p (o j) -> p o j", o=1).broadcast_to((128, 4, HF))
                    bk = bank.ap().rearrange("p (h l j) -> p h l j", h=4, l=2)
                    ev = evslice.rearrange("p (h l j) -> p h l j", h=4, l=2)
                    top, bot = bk[:, :, 0, :], bk[:, :, 1, :]
                    tv = [t.ap().rearrange("p (h j) -> p h j", h=4) for t in tmp]
                    sc.wait("vector", est, v_mm_done)
                    sc.wait("vector", esv, v_dep)
                    sc.op("vector", lambda E: E.tensor_tensor(out=tv[0], in0=top, in1=cP, op=ALU.mult))
                    sc.op("vector", lambda E: E.tensor_tensor(out=tv[1], in0=bot, in1=sP, op=ALU.mult))
                    sc.op("vector", lambda E: E.tensor_tensor(out=tv[2], in0=top, in1=sP, op=ALU.mult))
                    v4 = sc.op("vector", lambda E: E.tensor_tensor(out=tv[3], in0=bot, in1=cP, op=ALU.mult), esv)
                    sc.wait("vector", esv, v4)
                    sc.op("vector", lambda E: E.tensor_tensor(
                        out=ev[:, :, 0, :], in0=tv[0], in1=tv[1], op=ALU.subtract), esv)
                    return sc.op("vector", lambda E: E.tensor_tensor(
                        out=ev[:, :, 1, :], in0=tv[2], in1=tv[3], op=ALU.add), esv)

                v_rope_prev = 0
                passes = [
                    ("q", wqT, "tok", qtok, (qc, qs_), TOWN, ln1ownT),
                    ("k", wkT, "tok", ktok, (kc_, ks_), T, ln1kvT),
                    ("v", wvT, "planes", vsc, None, T, ln1kvT),
                ]
                for pname, wsrc, tmode, target, rtabs, ptoks, lnsrc in passes:
                    sc.wait("gpsimd", est, est.n)
                    for half in range(2):
                        sc.dma("gpsimd", sW,
                               W_sb.ap().rearrange("p (c f) -> p c f", c=EC)[:, :, ts(half, 1024)],
                               wsrc.rearrange("(c p) f -> p c f", p=128)[:, :, ts(half, 1024)])
                    vW = sW.n
                    ntb = ptoks // 2048
                    ev_hist = []
                    for tb in range(ntb):
                        sc.wait("sync", est, est.n)
                        for ec in range(EC):
                            sc.dma("sync", sLnT, lnT[:, ts(ec, 2048)],
                                   lnsrc[ec, :, ts(tb, 2048)])
                        sc.waitF("tensor", sLnT)
                        if tb == 0:
                            sc.wait("tensor", sW, vW)
                            sc.wait("tensor", esv, esv.n)
                        for t128 in range(16):
                            grp = t128 % 2
                            tile_global = tb * 16 + t128
                            if len(ev_hist) >= 2:
                                sc.wait("tensor", esv, ev_hist[-2])
                            v_mm = 0
                            for ec in range(EC):
                                lhs = lnT[:, ec * 2048 + t128 * 128: ec * 2048 + (t128 + 1) * 128]
                                for f in range(4):
                                    v_mm = sc.op("tensor", lambda E, grp=grp, f=f, lhs=lhs, ec=ec: E.matmul(
                                        psP1[grp * 4 + f][:], lhs,
                                        W_sb[:, ec * 2048 + f * 512: ec * 2048 + (f + 1) * 512],
                                        start=(ec == 0), stop=(ec == EC - 1)),
                                        est if ec == EC - 1 else None)
                            eb = evb[t128 % 2]
                            sc.waitF("vector", sSt[t128 % 2])
                            v_ev = 0
                            if rtabs is not None:
                                for f in range(4):
                                    v_ev = rope_evac(psP1[grp * 4 + f], eb[:, ts(f, 512)],
                                                     rtabs[0], rtabs[1], tile_global, v_mm,
                                                     max(v_tab, v_rope_prev))
                                    v_rope_prev = v_ev
                            else:
                                sc.wait("vector", est, v_mm)
                                for f in range(4):
                                    v_ev = sc.op("vector", lambda E, f=f, grp=grp, eb=eb: E.tensor_copy(
                                        eb[:, ts(f, 512)], psP1[grp * 4 + f][:]), esv)
                            sc.wait("gpsimd", esv, v_ev)
                            if tmode == "tok":
                                sc.dma("gpsimd", sSt[t128 % 2],
                                       target[ts(tile_global, 128), :], eb[:])
                            else:
                                for h in range(H):
                                    sc.dma("gpsimd", sSt[t128 % 2],
                                           target[h, ts(tile_global, 128), :],
                                           eb[:, ts(h, 128)])
                            ev_hist.append(v_ev)
                sc.waitF("gpsimd", sSt[0])
                sc.waitF("gpsimd", sSt[1])

                p1ps.close()
                sc.waitF("sync", sSt[0])
                sc.waitF("sync", sSt[1])
                psT = [e1(nc.psum_tensor(f"p1psT{i}", [128, 512], BF16)) for i in range(2)]
                s15 = [mksem("p15a"), mksem("p15b")]
                s15s = [mksem("p15s0"), mksem("p15s1")]
                evs15 = []
                trl = {}
                jobs = [(qtok, qTp, i) for i in range(NT128_OWN)] + \
                       [(ktok, kTp, i) for i in range(NT128_KV)]
                for idx, (src, dst, i) in enumerate(jobs):
                    bi = idx % 2
                    if idx >= 2:
                        sc.wait("sync", est, trl[idx - 2])
                    sc.dma("sync", s15[bi], evb[bi][:], src[ts(i, 128), :])
                    sc.waitF("tensor", s15[bi])
                    for q4 in range(4):
                        bk = psT[q4 % 2]
                        if len(evs15) >= 2:
                            sc.wait("tensor", esv, evs15[-2])
                        v_tr = 0
                        for j in range(4):
                            h = q4 * 4 + j
                            v_tr = sc.op("tensor", lambda E, h=h, bk=bk, bi=bi, j=j: E.transpose(
                                bk[:, ts(j, 128)], evb[bi][:, ts(h, 128)], ident_sb[:]), est)
                        if q4 == 3:
                            trl[idx] = v_tr
                        sc.wait("vector", est, v_tr)
                        if q4 == 0:
                            sc.waitF("vector", s15s[bi])
                        evs15.append(sc.op("vector", lambda E, q4=q4, bk=bk, bi=bi: E.tensor_copy(
                            lnT[:, bi * 8192 + q4 * 512: bi * 8192 + (q4 + 1) * 512], bk[:]), esv))
                    sc.wait("gpsimd", esv, evs15[-1])
                    for h in range(H):
                        sc.dma("gpsimd", s15s[bi], dst[h, :, ts(i, 128)],
                               lnT[:, bi * 8192 + h * 128: bi * 8192 + (h + 1) * 128])
                sc.waitF("gpsimd", s15s[0])
                sc.waitF("gpsimd", s15s[1])
                sc.barrier(bar)
        if nphases > 1:
            _phase1()

        def _phase2():
            with contextlib.ExitStack() as p2:
                e2 = p2.enter_context
                QT = e2(nc.sbuf_tensor("p2QT", [128, H * 512], BF16))
                KT = [e2(nc.sbuf_tensor(f"p2KT{i}", [128, T], BF16)) for i in range(2)]
                V = [e2(nc.sbuf_tensor(f"p2V{i}", [128, 32 * 128], BF16)) for i in range(2)]
                PT = [e2(nc.sbuf_tensor(f"p2PT{i}", [128, 512], BF16)) for i in range(4)]
                MK = e2(nc.sbuf_tensor("p2MK", [128, 8 * 512], BF16))
                YS = [e2(nc.sbuf_tensor(f"p2YS{i}", [128, 512], BF16)) for i in range(2)]
                SR = e2(nc.sbuf_tensor("p2SR", [1, 512], F32))
                SR2 = e2(nc.sbuf_tensor("p2SR2", [1, 512], F32))
                BC = e2(nc.sbuf_tensor("p2BC", [128, 512], F32))
                psS = [e2(nc.psum_tensor(f"p2psS{i}", [128, 512], F32)) for i in range(4)]
                psY = [e2(nc.psum_tensor(f"p2psY{i}", [128, 512], F32)) for i in range(2)]
                psSum = e2(nc.psum_tensor("p2psSum", [1, 512], F32))
                psBC = e2(nc.psum_tensor("p2psBC", [128, 512], F32))
                sQT = mksem("p2qt")
                sKV = [mksem("p2kv0"), mksem("p2kv1")]
                sYSt = [mksem("p2y0"), mksem("p2y1")]

                sc.waitF("scalar", sTab)
                sc.waitF("tensor", sTab)
                for g in range(NG):
                    Lg = 1024 * (g + 1)
                    KCg = 8 * (g + 1)
                    sc.dma("sync", sQT, MK[:], pmask[:, g * 8 * 512:(g + 1) * 8 * 512])
                    for h in range(H):
                        sc.dma("sync", sQT, QT[:, ts(h, 512)],
                               qTp[h, :, ts(g, 512)])
                    vQT = sQT.n
                    sc.wait("vector", sQT, vQT)
                    av_last = {}
                    y_vals = {}
                    for h in range(H):
                        hb = h % 2
                        if h >= 2:
                            sc.wait("scalar", est, av_last[h - 2])
                        sc.dma("scalar", sKV[hb], KT[hb][:, 0:Lg],
                               kTp[h, :, 0:Lg])
                        sc.dma("scalar", sKV[hb],
                               V[hb].ap().rearrange("p (c d) -> p c d", d=128)[:, 0:KCg, :],
                               vsc[h, 0:Lg, :].rearrange("(c p) d -> p c d", p=128))
                        if h == 0:
                            sc.wait("tensor", sQT, vQT)
                        sc.waitF("tensor", sKV[hb])
                        if h >= 2:
                            sc.wait("tensor", esv, y_vals[h - 2])
                        v_es = {}
                        av_done = {}
                        for kc in range(KCg):
                            sb = psS[kc % 4]
                            pb = PT[kc % 4]
                            if kc >= 4:
                                sc.wait("tensor", ess, v_es[kc - 4])
                            v_s = sc.op("tensor", lambda E, sb=sb, kc=kc, hb=hb, h=h: E.matmul(
                                sb[:], KT[hb][:, ts(kc, 128)], QT[:, ts(h, 512)],
                                start=True, stop=True), est)
                            sc.wait("scalar", est, v_s)
                            if kc >= 4:
                                sc.wait("scalar", est, av_done[kc - 4])
                            v_e = sc.op("scalar", lambda E, pb=pb, sb=sb: E.activation(
                                out=pb[:], in_=sb[:], func=AF.Exp, bias=negc0_sb[:]), ess)
                            v_es[kc] = v_e
                            dep_sem, dep_val = ess, v_e
                            if kc >= 8 * g:
                                sc.wait("vector", ess, v_e)
                                v_m = sc.op("vector", lambda E, pb=pb, kc=kc, g=g: E.tensor_tensor(
                                    out=pb[:], in0=pb[:], in1=MK[:, ts(kc - 8 * g, 512)],
                                    op=ALU.mult), esv)
                                dep_sem, dep_val = esv, v_m
                            sc.wait("tensor", dep_sem, dep_val)
                            sc.op("tensor", lambda E, pb=pb, kc=kc, hb=hb, KCg=KCg: E.matmul(
                                psY[hb][:], V[hb][:, ts(kc, 128)], pb[:],
                                start=(kc == 0), stop=(kc == KCg - 1)), est)
                            av_done[kc] = sc.op("tensor", lambda E, pb=pb, kc=kc, KCg=KCg: E.matmul(
                                psSum[:], onesc_sb[:], pb[:],
                                start=(kc == 0), stop=(kc == KCg - 1)), est)
                        av_last[h] = av_done[KCg - 1]
                        sc.wait("vector", est, av_last[h])
                        v_cp = sc.op("vector", lambda E: E.tensor_copy(SR[:], psSum[:]), esv)
                        sc.wait("vector", esv, v_cp)
                        v_sr = sc.op("vector", lambda E: E.reciprocal(SR2[:], SR[:]), esv)
                        sc.wait("tensor", esv, v_sr)
                        v_bc = sc.op("tensor", lambda E: E.matmul(
                            psBC[:], onesr_sb[:], SR2[:], start=True, stop=True), est)
                        sc.wait("vector", est, v_bc)
                        v_bcc = sc.op("vector", lambda E: E.tensor_copy(BC[:], psBC[:]), esv)
                        sc.waitF("vector", sYSt[hb])
                        sc.wait("vector", esv, v_bcc)
                        v_y = sc.op("vector", lambda E, hb=hb: E.tensor_tensor(
                            out=YS[hb][:], in0=psY[hb][:], in1=BC[:], op=ALU.mult), esv)
                        y_vals[h] = v_y
                        sc.wait("gpsimd", esv, v_y)
                        sc.dma("gpsimd", sYSt[hb], ysc[h, :, ts(g, 512)], YS[hb][:])
                    sc.wait("sync", est, est.n)
                sc.waitF("gpsimd", sYSt[0])
                sc.waitF("gpsimd", sYSt[1])
                sc.barrier(bar)
        if nphases > 2:
            _phase2()

        def _phase3():
            with contextlib.ExitStack() as p3:
                e3 = p3.enter_context
                Y = e3(nc.sbuf_tensor("p3Y", [128, H * 2048], BF16))
                WP = [e3(nc.sbuf_tensor(f"p3WP{i}", [128, EC * 128], BF16)) for i in range(2)]
                AT = e3(nc.sbuf_tensor("p3AT", [128, EC * 2048], BF16))
                p3ps = contextlib.ExitStack()
                psP3 = [p3ps.enter_context(nc.psum_tensor(f"p3ps{i}", [128, 512], F32))
                        for i in range(8)]
                sY = mksem("p3y")
                sWP = [mksem("p3w0"), mksem("p3w1")]

                sc.waitF("tensor", sTab)
                for h in range(H):
                    sc.dma("sync", sY, Y[:, ts(h, 2048)], ysc[h, :, :])
                vY = sY.n
                ev_hist = []
                mm_hist = []
                for eo in range(EC):
                    bi = eo % 2
                    if eo >= 2:
                        sc.wait("gpsimd", est, mm_hist[eo - 2])
                    sc.dma("gpsimd", sWP[bi],
                           WP[bi].ap().rearrange("p (c d) -> p c d", c=EC),
                           wprojC[eo].rearrange("(c p) d -> p c d", p=128))
                    sc.waitF("tensor", sWP[bi])
                    if eo == 0:
                        sc.wait("tensor", sY, vY)
                    grp = eo % 2
                    if len(ev_hist) >= 2:
                        sc.wait("tensor", esv, ev_hist[-2])
                    v_mm = 0
                    for h in range(H):
                        wl = WP[bi].ap().rearrange("p (c d) -> p c d", c=EC)[:, h, :]
                        for t in range(4):
                            v_mm = sc.op("tensor", lambda E, grp=grp, t=t, h=h, wl=wl: E.matmul(
                                psP3[grp * 4 + t][:], wl,
                                Y[:, h * 2048 + t * 512: h * 2048 + (t + 1) * 512],
                                start=(h == 0), stop=(h == H - 1)),
                                est if h == H - 1 else None)
                    mm_hist.append(v_mm)
                    sc.wait("vector", est, v_mm)
                    v_ev = 0
                    for t in range(4):
                        v_ev = sc.op("vector", lambda E, grp=grp, t=t, eo=eo: E.tensor_copy(
                            AT[:, eo * 2048 + t * 512: eo * 2048 + (t + 1) * 512],
                            psP3[grp * 4 + t][:]), esv)
                    ev_hist.append(v_ev)
                v_at_done = ev_hist[-1]

                xb = e3(nc.sbuf_tensor("p3xb", [128, E], F32))
                x1b = e3(nc.sbuf_tensor("p3x1", [128, E], F32))
                atok = e3(nc.sbuf_tensor("p3atok", [128, E], BF16))
                ln2b = e3(nc.sbuf_tensor("p3ln2", [128, E], BF16))
                l2buf = e3(nc.sbuf_tensor("p3l2buf", [128, E], BF16))
                stats = mkstats(e3, "p3")
                p3ps.close()
                psT = [e3(nc.psum_tensor(f"p3psT{i}", [128, 512], BF16)) for i in range(2)]
                sX3 = mksem("p3x")
                sX1St = mksem("p3x1st")
                sL2St = mksem("p3l2st")

                sc.wait("tensor", esv, v_at_done)
                prev = {}
                for i in range(NT128_OWN):
                    if i >= 1:
                        sc.wait("sync", esv, prev["add"])
                    sc.dma("sync", sX3, xb[:], x_own[ts(i, 128), :])
                    tr_ev = {}
                    for qtr in range(4):
                        bk = psT[qtr % 2]
                        if qtr >= 2:
                            sc.wait("tensor", esv, tr_ev[qtr - 2])
                        if i >= 1 and qtr < 2:
                            sc.wait("tensor", esv, prev[("trev", qtr + 2)])
                        v_tr = 0
                        for j in range(4):
                            eo = qtr * 4 + j
                            v_tr = sc.op("tensor", lambda E, eo=eo, bk=bk, i=i, j=j: E.transpose(
                                bk[:, ts(j, 128)],
                                AT[:, eo * 2048 + i * 128: eo * 2048 + (i + 1) * 128],
                                ident_sb[:]), est)
                        sc.wait("vector", est, v_tr)
                        if qtr == 0 and i >= 1:
                            sc.wait("vector", esv, prev["add"])
                        tr_ev[qtr] = sc.op("vector", lambda E, qtr=qtr, bk=bk: E.tensor_copy(
                            atok[:, ts(qtr, 512)], bk[:]), esv)
                    sc.waitF("vector", sX3)
                    if i >= 1:
                        sc.wait("vector", ess, prev["ln2"])
                        sc.waitF("vector", sX1St)
                    sc.wait("vector", esv, tr_ev[3])
                    v_add = sc.op("vector", lambda E: E.tensor_tensor(
                        out=x1b[:], in0=xb[:], in1=atok[:], op=ALU.add), esv)
                    v_ln2 = ln_tile(x1b, ln2b, stats, esv, v_add)
                    sc.wait("gpsimd", esv, v_add)
                    sc.dma("gpsimd", sX1St, x1sc[ts(i, 128), :], x1b[:])
                    sc.wait("tensor", ess, v_ln2)
                    l2evs = []
                    for q4 in range(4):
                        bk = psT[q4 % 2]
                        sc.wait("tensor", esv, tr_ev[q4 % 2 + 2] if q4 < 2 else l2evs[q4 - 2])
                        v_tr = 0
                        for j in range(4):
                            ec = q4 * 4 + j
                            v_tr = sc.op("tensor", lambda E, ec=ec, bk=bk, j=j: E.transpose(
                                bk[:, ts(j, 128)], ln2b[:, ts(ec, 128)], ident_sb[:]), est)
                        if q4 == 3:
                            v_l2tr = v_tr
                        sc.wait("vector", est, v_tr)
                        if q4 == 0:
                            sc.waitF("vector", sL2St)
                        l2evs.append(sc.op("vector", lambda E, q4=q4, bk=bk: E.tensor_copy(
                            l2buf[:, ts(q4, 512)], bk[:]), esv))
                    sc.wait("gpsimd", esv, l2evs[-1])
                    for ec in range(EC):
                        sc.dma("gpsimd", sL2St, ln2T[ec, :, ts(i, 128)],
                               l2buf[:, ts(ec, 128)])
                    sc.wait("scalar", est, v_l2tr)
                    prev = {"add": v_add, "ln2": v_ln2,
                            ("trev", 2): l2evs[2], ("trev", 3): l2evs[3]}
                sc.waitF("gpsimd", sX1St)
                sc.waitF("gpsimd", sL2St)
                sc.barrier(bar)
        if nphases > 3:
            _phase3()

        def _phase4():
            with contextlib.ExitStack() as p4:
                e4 = p4.enter_context
                LNT = e4(nc.sbuf_tensor("p4lnT", [128, EC * 2048], BF16))
                GW = [e4(nc.sbuf_tensor(f"p4gw{i}", [128, EC * 128], BF16)) for i in range(2)]
                UW = [e4(nc.sbuf_tensor(f"p4uw{i}", [128, EC * 128], BF16)) for i in range(2)]
                SIL = [e4(nc.sbuf_tensor(f"p4sil{i}", [128, 512], F32)) for i in range(2)]
                HB = e4(nc.sbuf_tensor("p4hb", [128, 2048], BF16))
                psP4 = [e4(nc.psum_tensor(f"p4ps{i}", [128, 512], F32)) for i in range(8)]
                sLnT = mksem("p4lnt")
                sGW = [mksem("p4g0"), mksem("p4g1")]
                sHSt = mksem("p4h")

                for ec in range(EC):
                    sc.dma("sync", sLnT, LNT[:, ts(ec, 2048)], ln2T[ec, :, :])
                vLn = sLnT.n
                sc.waitF("scalar", sTab)
                sc.waitF("vector", sTab)
                hist = []
                sil_hist = []
                for ic in range(IC):
                    bi = ic % 2
                    if ic >= 2:
                        sc.wait("gpsimd", est, hist[ic - 2][0])
                    sc.dma("gpsimd", sGW[bi], GW[bi].ap().rearrange("p (c d) -> p c d", c=EC),
                           gateC[ic].rearrange("(c p) d -> p c d", p=128))
                    sc.dma("gpsimd", sGW[bi], UW[bi].ap().rearrange("p (c d) -> p c d", c=EC),
                           upC[ic].rearrange("(c p) d -> p c d", p=128))
                    sc.waitF("tensor", sGW[bi])
                    if ic == 0:
                        sc.wait("tensor", sLnT, vLn)
                    if ic >= 1:
                        sc.wait("tensor", esv, hist[ic - 1][1])
                    v_mm = 0
                    for ec in range(EC):
                        gl = GW[bi].ap().rearrange("p (c d) -> p c d", c=EC)[:, ec, :]
                        ul = UW[bi].ap().rearrange("p (c d) -> p c d", c=EC)[:, ec, :]
                        for t in range(4):
                            v_mm = sc.op("tensor", lambda E, gl=gl, ec=ec, t=t: E.matmul(
                                psP4[t][:], gl,
                                LNT[:, ec * 2048 + t * 512: ec * 2048 + (t + 1) * 512],
                                start=(ec == 0), stop=(ec == EC - 1)),
                                est if ec == EC - 1 else None)
                        for t in range(4):
                            v_mm = sc.op("tensor", lambda E, ul=ul, ec=ec, t=t: E.matmul(
                                psP4[4 + t][:], ul,
                                LNT[:, ec * 2048 + t * 512: ec * 2048 + (t + 1) * 512],
                                start=(ec == 0), stop=(ec == EC - 1)),
                                est if ec == EC - 1 else None)
                    if ic >= 1:
                        sc.waitF("vector", sHSt)
                    v_ev = 0
                    for t in range(4):
                        sb_sil = SIL[t % 2]
                        sc.wait("scalar", est, v_mm)
                        if len(sil_hist) >= 2:
                            sc.wait("scalar", esv, sil_hist[-2])
                        v_sil = sc.op("scalar", lambda E, t=t, ic=ic, sb_sil=sb_sil: E.activation(
                            out=sb_sil[:], in_=psP4[t][:], func=AF.Silu,
                            scale=gs_sb[:, ic:ic + 1]), ess)
                        sc.wait("vector", ess, v_sil)
                        v_ev = sc.op("vector", lambda E, t=t, ic=ic, sb_sil=sb_sil: E.scalar_tensor_tensor(
                            out=HB[:, ts(t, 512)], in0=psP4[4 + t][:],
                            scalar=us_sb[:, ic:ic + 1], in1=sb_sil[:],
                            op0=ALU.mult, op1=ALU.mult), esv)
                        sil_hist.append(v_ev)
                    sc.wait("gpsimd", esv, v_ev)
                    sc.dma("gpsimd", sHSt, hsc[ic, :, :], HB[:])
                    hist.append((v_mm, v_ev))
                sc.waitF("gpsimd", sHSt)
                sc.barrier(bar)
        if nphases > 4:
            _phase4()

        def _phase5():
            with contextlib.ExitStack() as p5:
                e5 = p5.enter_context
                HH = e5(nc.sbuf_tensor("p5H", [128, IC * 1024], BF16))
                DW = [e5(nc.sbuf_tensor(f"p5dw{i}", [128, IC * 128], BF16)) for i in range(2)]
                FO = e5(nc.sbuf_tensor("p5fo", [128, EC * 1024], BF16))
                x1b5 = e5(nc.sbuf_tensor("p5x1", [128, E], F32))
                ftok = e5(nc.sbuf_tensor("p5ftok", [128, E], BF16))
                ob = e5(nc.sbuf_tensor("p5ob", [128, E], F32))
                psP5 = [e5(nc.psum_tensor(f"p5ps{i}", [128, 512], F32)) for i in range(4)]
                psT = [e5(nc.psum_tensor(f"p5psT{i}", [128, 512], BF16)) for i in range(2)]
                sH = mksem("p5h")
                sDW = [mksem("p5w0"), mksem("p5w1")]
                sX15 = mksem("p5x1")
                sOut = mksem("p5out")

                sc.waitF("scalar", sTab)
                for half in range(2):
                    sc.wait("sync", est, est.n)
                    sc.wait("gpsimd", est, est.n)
                    sc.wait("tensor", ess, ess.n)
                    sc.wait("tensor", esv, esv.n)
                    for ic in range(IC):
                        sc.dma("sync", sH, HH[:, ts(ic, 1024)],
                               hsc[ic, :, ts(half, 1024)])
                    vH = sH.n
                    ev_hist = []
                    mm_hist = []
                    for eo in range(EC):
                        bi = eo % 2
                        if eo >= 2:
                            sc.wait("gpsimd", est, mm_hist[eo - 2])
                        sc.dma("gpsimd", sDW[bi],
                               DW[bi].ap().rearrange("p (c d) -> p c d", c=IC),
                               downC[eo].rearrange("(c p) d -> p c d", p=128))
                        sc.waitF("tensor", sDW[bi])
                        if eo == 0:
                            sc.wait("tensor", sH, vH)
                        if len(ev_hist) >= 2:
                            sc.wait("tensor", ess, ev_hist[-2])
                        grp = eo % 2
                        v_mm = 0
                        for ic in range(IC):
                            dl = DW[bi].ap().rearrange("p (c d) -> p c d", c=IC)[:, ic, :]
                            for t in range(2):
                                v_mm = sc.op("tensor", lambda E, dl=dl, ic=ic, t=t, grp=grp: E.matmul(
                                    psP5[grp * 2 + t][:], dl,
                                    HH[:, ic * 1024 + t * 512: ic * 1024 + (t + 1) * 512],
                                    start=(ic == 0), stop=(ic == IC - 1)),
                                    est if ic == IC - 1 else None)
                        mm_hist.append(v_mm)
                        sc.wait("scalar", est, v_mm)
                        v_ev = 0
                        for t in range(2):
                            v_ev = sc.op("scalar", lambda E, eo=eo, t=t, grp=grp: E.activation(
                                out=FO[:, eo * 1024 + t * 512: eo * 1024 + (t + 1) * 512],
                                in_=psP5[grp * 2 + t][:], func=AF.Copy,
                                scale=ds_sb[:, eo:eo + 1]), ess)
                        ev_hist.append(v_ev)
                    v_fo = ev_hist[-1]
                    sc.wait("tensor", ess, v_fo)
                    if half == 0:
                        prev = {}
                    for ii in range(8):
                        i = half * 8 + ii
                        if prev.get("o"):
                            sc.wait("sync", esv, prev["o"])
                        sc.dma("sync", sX15, x1b5[:], x1sc[ts(i, 128), :])
                        tr_ev = {}
                        for qtr in range(4):
                            bk = psT[qtr % 2]
                            if qtr >= 2:
                                sc.wait("tensor", esv, tr_ev[qtr - 2])
                            if qtr < 2 and prev.get(("trev", qtr + 2)):
                                sc.wait("tensor", esv, prev[("trev", qtr + 2)])
                            v_tr = 0
                            for j in range(4):
                                eo = qtr * 4 + j
                                v_tr = sc.op("tensor", lambda E, eo=eo, bk=bk, ii=ii, j=j: E.transpose(
                                    bk[:, ts(j, 128)],
                                    FO[:, eo * 1024 + ii * 128: eo * 1024 + (ii + 1) * 128],
                                    ident_sb[:]), est)
                            sc.wait("vector", est, v_tr)
                            if qtr == 0 and prev.get("o"):
                                sc.wait("vector", esv, prev["o"])
                            tr_ev[qtr] = sc.op("vector", lambda E, qtr=qtr, bk=bk: E.tensor_copy(
                                ftok[:, ts(qtr, 512)], bk[:]), esv)
                        sc.waitF("vector", sX15)
                        sc.waitF("vector", sOut)
                        sc.wait("vector", esv, tr_ev[3])
                        v_o = sc.op("vector", lambda E: E.tensor_tensor(
                            out=ob[:], in0=x1b5[:], in1=ftok[:], op=ALU.add), esv)
                        sc.wait("sync", esv, v_o)
                        sc.dma("sync", sOut, out[ts(i, 128), :], ob[:])
                        prev = {"o": v_o, ("trev", 2): tr_ev[2], ("trev", 3): tr_ev[3]}
                sc.waitF("sync", sOut)
        if nphases > 5:
            _phase5()

        block = ecm(nc.Block())

        @block.tensor
        def _(Eng):
            for f in sc.q["tensor"]:
                f(Eng)

        @block.vector
        def _(Eng):
            for f in sc.q["vector"]:
                f(Eng)

        @block.scalar
        def _(Eng):
            for f in sc.q["scalar"]:
                f(Eng)

        @block.sync
        def _(Eng):
            for f in sc.q["sync"]:
                f(Eng)

        @block.gpsimd
        def _(Eng):
            for f in sc.q["gpsimd"]:
                f(Eng)

    return nc


def _host_tables(parity):
    inv = 1.0 / (10000.0 ** (np.arange(0, D, 2, dtype=np.float64) / D))

    def tab(tvals, scale):
        ang = tvals.reshape(-1, 1) * inv[None, :]
        return np.cos(ang) * scale, np.sin(ang) * scale

    tk = (np.arange(NT128_KV)[None, :] * 128 + np.arange(128)[:, None]).astype(np.float64)
    ck, sk = tab(tk.reshape(-1), 1.0)
    ck = ck.reshape(128, NT128_KV, HF).reshape(128, -1)
    sk = sk.reshape(128, NT128_KV, HF).reshape(128, -1)
    tq = (128 * (2 * np.arange(NSLOT)[None, :] + parity) + np.arange(128)[:, None]).astype(np.float64)
    cq, sq = tab(tq.reshape(-1), 1.0 / np.sqrt(np.float64(D)))
    cq = cq.reshape(128, NSLOT, HF).reshape(128, -1)
    sq = sq.reshape(128, NSLOT, HF).reshape(128, -1)

    def hilo(a):
        hi = a.astype(np.float32)
        lo = (a - hi.astype(np.float64)).astype(np.float32)
        return np.ascontiguousarray(hi), np.ascontiguousarray(lo)
    return [hilo(x) for x in (cq, sq, ck, sk)]


def _host_masks(parity):
    m = np.zeros((128, NG, 8, 512), dtype=np.float32)
    kk = np.arange(128)[:, None]
    for g in range(NG):
        for kc in range(8):
            kglob = 1024 * g + 128 * kc + kk
            sl = 4 * g + np.arange(512)[None, :] // 128
            qglob = 256 * sl + 128 * parity + (np.arange(512)[None, :] % 128)
            m[:, g, kc, :] = (kglob <= qglob).astype(np.float32)
    return np.ascontiguousarray(m.reshape(128, -1)).astype(BF)


def _qk_perm():
    p = np.zeros(D, dtype=np.int64)
    for l in range(2):
        for j in range(HF):
            p[l * HF + j] = 2 * j + l
    return p


def _host_weights(Wqkv, Wproj, gw, uw, dw):
    blk = _qk_perm()
    perm_h = np.concatenate([h * D + blk for h in range(H)])
    WT = Wqkv.T
    wqT = np.ascontiguousarray(WT[:, 0:E][:, perm_h])
    wkT = np.ascontiguousarray(WT[:, E:2 * E][:, perm_h])
    wvT = np.ascontiguousarray(WT[:, 2 * E:3 * E])
    wprojC = np.ascontiguousarray(Wproj.T.reshape(E, EC, 128).transpose(1, 0, 2))
    gateC = np.ascontiguousarray(gw.T.reshape(E, IC, 128).transpose(1, 0, 2))
    upC = np.ascontiguousarray(uw.T.reshape(E, IC, 128).transpose(1, 0, 2))
    downC = np.ascontiguousarray(dw.T.reshape(I, EC, 128).transpose(1, 0, 2))
    return wqT, wkT, wvT, wprojC, gateC, upC, downC


def _host_inmap(core, x, wqT, wkT, wvT, wprojC, gateC, upC, downC,
                gs, us, ds, tables, masks, own_rows):
    b, p = core // 2, core % 2
    (cq_h, cq_l), (sq_h, sq_l), (ck_h, ck_l), (sk_h, sk_l) = tables[p]
    return {
        "x_kv": x[b], "x_own": np.ascontiguousarray(x[b][own_rows[p]]),
        "wqT": wqT, "wkT": wkT, "wvT": wvT,
        "wprojC": wprojC, "gateC": gateC, "upC": upC, "downC": downC,
        "gate_s": np.ascontiguousarray(gs.reshape(IC, 128).T),
        "up_s": np.ascontiguousarray(us.reshape(IC, 128).T),
        "down_s": np.ascontiguousarray(ds.reshape(EC, 128).T),
        "qcos_hi": cq_h, "qcos_lo": cq_l, "qsin_hi": sq_h, "qsin_lo": sq_l,
        "kcos_hi": ck_h, "kcos_lo": ck_l, "ksin_hi": sk_h, "ksin_lo": sk_l,
        "pmask": masks[p],
        "ident_bf": np.eye(128, dtype=np.float32).astype(BF),
        "ones_col_bf": np.ones((128, 1), np.float32).astype(BF),
        "ones_row_f": np.ones((1, 128), np.float32),
        "negc0": np.full((128, 1), -C0, np.float32),
    }


def kernel(L0_attn_x, L0_attn_W_qkv, L0_attn_W_proj, L0_ffn_gate_w, L0_ffn_up_w,
           L0_ffn_down_w, L0_ffn_gate_s, L0_ffn_up_s, L0_ffn_down_s):
    x = np.ascontiguousarray(np.asarray(L0_attn_x, dtype=np.float32))
    Wqkv = np.asarray(L0_attn_W_qkv, dtype=np.float32)
    Wproj = np.asarray(L0_attn_W_proj, dtype=np.float32)
    gw = np.asarray(L0_ffn_gate_w, dtype=np.float32)
    uw = np.asarray(L0_ffn_up_w, dtype=np.float32)
    dw = np.asarray(L0_ffn_down_w, dtype=np.float32)
    gs = np.asarray(L0_ffn_gate_s, dtype=np.float32)
    us = np.asarray(L0_ffn_up_s, dtype=np.float32)
    ds = np.asarray(L0_ffn_down_s, dtype=np.float32)

    wqT, wkT, wvT, wprojC, gateC, upC, downC = _host_weights(Wqkv, Wproj, gw, uw, dw)
    tables = [_host_tables(p) for p in range(2)]
    masks = [_host_masks(p) for p in range(2)]
    own_rows = [np.concatenate([np.arange(128 * (2 * j + p), 128 * (2 * j + p) + 128)
                                for j in range(NSLOT)]) for p in range(2)]

    nc = build_program()
    in_maps = [_host_inmap(c, x, wqT, wkT, wvT, wprojC, gateC, upC, downC,
                           gs, us, ds, tables, masks, own_rows)
               for c in range(8)]
    res = run_bass_kernel_spmd(nc, in_maps, core_ids=list(range(8)))
    outp = np.empty((B, T, E), dtype=np.float32)
    for c in range(8):
        b, p = c // 2, c % 2
        outp[b][own_rows[p]] = res.results[c]["out"]
    return outp


# version 02 (final)
# speedup vs baseline: 1.2172x; 1.2172x over previous
import contextlib
import numpy as np
import ml_dtypes
import concourse.bass as bass
import concourse.mybir as mybir
from concourse.bass_utils import run_bass_kernel_spmd

F32 = mybir.dt.float32
BF16 = mybir.dt.bfloat16
AF = mybir.ActivationFunctionType
ALU = mybir.AluOpType
AX = mybir.AxisListType

B, T, E, H, D = 4, 4096, 2048, 16, 128
HF = 64
FQ = 6144
I = 5632
IC = I // 128
EC = E // 128
TOWN = 2048
NSLOT = 16
NG = 4
NT128_KV = T // 128
NT128_OWN = TOWN // 128
C0 = 14.0
EPS = 1e-5
INV_E = 1.0 / 2048.0
BF = ml_dtypes.bfloat16

ENGS = ("tensor", "vector", "scalar", "sync", "gpsimd")


class Sem:
    def __init__(self, handle, name):
        self.h = handle
        self.name = name
        self.n = 0

    def inc(self, amt):
        self.n += amt
        return self.n


class Sched:

    def __init__(self):
        self.q = {e: [] for e in ENGS}

    def add(self, eng, fn):
        self.q[eng].append(fn)

    def dma(self, eng, sem, out, in_, transpose=False):
        v = sem.inc(16)

        def f(E, out=out, in_=in_, sem=sem, transpose=transpose):
            E.dma_start(out=out, in_=in_, transpose=transpose).then_inc(sem.h, 16)
        self.add(eng, f)
        return v

    def wait(self, eng, sem, val):
        assert val <= sem.n, f"wait on {sem.name} for {val} > planned {sem.n}"

        def f(E, sem=sem, val=val):
            E.wait_ge(sem.h, val)
        self.add(eng, f)

    def waitF(self, eng, sem):
        self.wait(eng, sem, sem.n)

    def op(self, eng, fn, sem=None):
        if sem is None:
            self.add(eng, lambda E, fn=fn: fn(E))
            return None
        v = sem.inc(1)

        def f(E, fn=fn, sem=sem):
            fn(E).then_inc(sem.h, 1)
        self.add(eng, f)
        return v

    def vnop(self):
        self.add("vector", lambda E: E.nop())

    def barrier(self, bar):
        target = bar.n + len(ENGS)
        for e in ENGS:
            def f(E, bar=bar):
                E.nop().then_inc(bar.h, 1)
            self.add(e, f)
        bar.n = target
        for e in ENGS:
            self.wait(e, bar, target)


def ts(i, w):
    return slice(i * w, (i + 1) * w)


def build_program(debug=False, nphases=6):
    nc = bass.Bass("TRN2", target_bir_lowering=False)

    def din(name, shape, dt=F32):
        return nc.dram_tensor(name, list(shape), dt, kind="ExternalInput").ap()

    x_kv = din("x_kv", (T, E))
    x_own = din("x_own", (TOWN, E))
    wqT = din("wqT", (E, E))
    wkT = din("wkT", (E, E))
    wvT = din("wvT", (E, E))
    wprojC = din("wprojC", (EC, E, 128))
    gateC = din("gateC", (IC, E, 128))
    upC = din("upC", (IC, E, 128))
    downC = din("downC", (EC, I, 128))
    gate_s = din("gate_s", (128, IC))
    up_s = din("up_s", (128, IC))
    down_s = din("down_s", (128, EC))
    qcos_hi = din("qcos_hi", (128, NSLOT * HF))
    qcos_lo = din("qcos_lo", (128, NSLOT * HF))
    qsin_hi = din("qsin_hi", (128, NSLOT * HF))
    qsin_lo = din("qsin_lo", (128, NSLOT * HF))
    kcos_hi = din("kcos_hi", (128, NT128_KV * HF))
    kcos_lo = din("kcos_lo", (128, NT128_KV * HF))
    ksin_hi = din("ksin_hi", (128, NT128_KV * HF))
    ksin_lo = din("ksin_lo", (128, NT128_KV * HF))
    pmask = din("pmask", (128, NG * 8 * 512), BF16)
    ident_bf = din("ident_bf", (128, 128), BF16)
    ones_col_bf = din("ones_col_bf", (128, 1), BF16)
    ones_row_f = din("ones_row_f", (1, 128))
    negc0 = din("negc0", (128, 1))

    out = nc.dram_tensor("out", [TOWN, E], F32, kind="ExternalOutput").ap()

    skind = "ExternalOutput" if debug else "Internal"
    ln1kvT = nc.dram_tensor("s_ln1kvT", [EC, 128, T], BF16, kind=skind).ap()
    ln1ownT = nc.dram_tensor("s_ln1ownT", [EC, 128, TOWN], BF16, kind=skind).ap()
    qtok = nc.dram_tensor("s_q", [TOWN, E], BF16, kind=skind).ap()
    ktok = nc.dram_tensor("s_k", [T, E], BF16, kind=skind).ap()
    qTp = nc.dram_tensor("s_qT", [H, D, TOWN], BF16, kind=skind).ap()
    kTp = nc.dram_tensor("s_kT", [H, D, T], BF16, kind=skind).ap()
    vsc = nc.dram_tensor("s_v", [H, T, D], BF16, kind=skind).ap()
    ysc = nc.dram_tensor("s_y", [H, D, TOWN], BF16, kind=skind).ap()
    x1sc = nc.dram_tensor("s_x1", [TOWN, E], F32, kind=skind).ap()
    ln2T = nc.dram_tensor("s_ln2T", [EC, 128, TOWN], BF16, kind=skind).ap()
    hsc = nc.dram_tensor("s_h", [IC, 128, TOWN], BF16, kind=skind).ap()

    sc = Sched()

    with contextlib.ExitStack() as stk:
        ecm = stk.enter_context

        def mksem(name):
            return Sem(ecm(nc.semaphore(name)), name)

        est, esv, ess, bar = mksem("est"), mksem("esv"), mksem("ess"), mksem("bar")
        sTab = mksem("sTab")

        ident_sb = ecm(nc.sbuf_tensor("ident_sb", [128, 128], BF16))
        onesc_sb = ecm(nc.sbuf_tensor("onesc_sb", [128, 1], BF16))
        onesr_sb = ecm(nc.sbuf_tensor("onesr_sb", [1, 128], F32))
        gs_sb = ecm(nc.sbuf_tensor("gs_sb", [128, IC], F32))
        us_sb = ecm(nc.sbuf_tensor("us_sb", [128, IC], F32))
        ds_sb = ecm(nc.sbuf_tensor("ds_sb", [128, EC], F32))
        negc0_sb = ecm(nc.sbuf_tensor("negc0_sb", [128, 1], F32))

        for dst, src in [(ident_sb, ident_bf), (onesc_sb, ones_col_bf),
                         (onesr_sb, ones_row_f), (gs_sb, gate_s),
                         (us_sb, up_s), (ds_sb, down_s), (negc0_sb, negc0)]:
            sc.dma("sync", sTab, dst[:], src[:])

        def ln_tile(xbuf, lnbuf, stats, dep_sem, dep_val):
            s1, ssq, m_, var_, rstd, rstd2, mb, sqs = (
                stats["s1"], stats["ssq"], stats["m"], stats["var"],
                stats["rstd"], stats["rstd2"], stats["mb"], stats["sqscr"])
            sc.wait("scalar", dep_sem, dep_val)
            v_sq = sc.op("scalar", lambda E: E.activation(
                out=sqs[:], in_=xbuf[:], func=AF.Square, accum_out=ssq[:]), ess)
            sc.wait("vector", dep_sem, dep_val)
            last = sc.op("vector", lambda E: E.reduce_sum(s1[:], xbuf[:], axis=AX.X), esv)

            def step(fn, extra=None):
                nonlocal last
                sc.wait("vector", esv, last)
                if extra is not None:
                    sc.wait("vector", extra[0], extra[1])
                last = sc.op("vector", fn, esv)
            step(lambda E: E.tensor_scalar_mul(m_[:], s1[:], INV_E), (ess, v_sq))
            step(lambda E: E.tensor_scalar_mul(var_[:], ssq[:], INV_E))
            step(lambda E: E.tensor_tensor(out=s1[:], in0=m_[:], in1=m_[:], op=ALU.mult))
            step(lambda E: E.tensor_tensor(out=var_[:], in0=var_[:], in1=s1[:], op=ALU.subtract))
            step(lambda E: E.tensor_scalar_add(var_[:], var_[:], EPS))
            v_var = last
            sc.wait("scalar", esv, v_var)
            v_sd = sc.op("scalar", lambda E: E.activation(
                out=rstd[:], in_=var_[:], func=AF.Sqrt), ess)
            step(lambda E: E.reciprocal(rstd2[:], rstd[:]), (ess, v_sd))
            step(lambda E: E.tensor_tensor(out=mb[:], in0=m_[:], in1=rstd2[:], op=ALU.mult))
            step(lambda E: E.tensor_scalar_mul(mb[:], mb[:], -1.0))
            v_mb = last
            sc.wait("scalar", esv, v_mb)
            v_ln = sc.op("scalar", lambda E: E.activation(
                out=lnbuf[:], in_=xbuf[:], func=AF.Identity, bias=mb[:], scale=rstd2[:]), ess)
            return v_ln

        def mkstats(e, pfx):
            st = {k: e(nc.sbuf_tensor(f"{pfx}_{k}", [128, 1], F32))
                  for k in ("s1", "ssq", "m", "var", "rstd", "rstd2", "mb")}
            st["sqscr"] = e(nc.sbuf_tensor(f"{pfx}_sqscr", [128, E], F32))
            return st

        def _phase0():
            with contextlib.ExitStack() as p0:
                e0 = p0.enter_context
                xb = [e0(nc.sbuf_tensor(f"p0x{i}", [128, E], F32)) for i in range(2)]
                lb = [e0(nc.sbuf_tensor(f"p0l{i}", [128, E], BF16)) for i in range(2)]
                tb_ = [e0(nc.sbuf_tensor(f"p0t{i}", [128, E], BF16)) for i in range(2)]
                stats = mkstats(e0, "p0")
                psT = [e0(nc.psum_tensor(f"p0psT{i}", [128, 512], BF16)) for i in range(2)]
                sX = [mksem("p0x0"), mksem("p0x1")]
                sSt = [mksem("p0s0"), mksem("p0s1")]

                tiles = [(ln1ownT, x_own, i) for i in range(NT128_OWN)] + \
                        [(ln1kvT, x_kv, i) for i in range(NT128_KV)]
                hist = []
                prev_ev = {}
                evs = []
                for idx, (dst, src, i) in enumerate(tiles):
                    bi = idx % 2
                    if idx >= 2:
                        sc.wait("sync", ess, hist[idx - 2])
                    sc.dma("sync", sX[bi], xb[bi][:], src[ts(i, 128), :])
                    if idx >= 2:
                        sc.wait("scalar", est, prev_ev[(idx - 2, "trlast")])
                    v_ln = ln_tile(xb[bi], lb[bi], stats, sX[bi], sX[bi].n)
                    sc.wait("tensor", ess, v_ln)
                    sc.waitF("tensor", sTab)
                    for q4 in range(4):
                        bk = psT[q4 % 2]
                        if len(evs) >= 2:
                            sc.wait("tensor", esv, evs[-2])
                        v_tr = 0
                        for j in range(4):
                            ec = q4 * 4 + j
                            v_tr = sc.op("tensor", lambda E, ec=ec, bk=bk, bi=bi, j=j: E.transpose(
                                bk[:, ts(j, 128)], lb[bi][:, ts(ec, 128)], ident_sb[:]), est)
                        if q4 == 3:
                            prev_ev[(idx, "trlast")] = v_tr
                        sc.wait("vector", est, v_tr)
                        if q4 == 0:
                            sc.waitF("vector", sSt[bi])
                        evs.append(sc.op("vector", lambda E, q4=q4, bk=bk, bi=bi: E.tensor_copy(
                            tb_[bi][:, ts(q4, 512)], bk[:]), esv))
                    sc.wait("gpsimd", esv, evs[-1])
                    for ec in range(EC):
                        sc.dma("gpsimd", sSt[bi], dst[ec, :, ts(i, 128)],
                               tb_[bi][:, ts(ec, 128)])
                    hist.append(v_ln)
                sc.waitF("gpsimd", sSt[0])
                sc.waitF("gpsimd", sSt[1])
                sc.barrier(bar)
        if nphases > 0:
            _phase0()

        def _phase1():
            with contextlib.ExitStack() as p1:
                e1 = p1.enter_context
                W_sb = e1(nc.sbuf_tensor("p1W", [128, EC * 2048], BF16))
                lnT = e1(nc.sbuf_tensor("p1lnT", [128, EC * 2048], BF16))
                qc = e1(nc.sbuf_tensor("p1qc", [128, NSLOT * HF], F32))
                qs_ = e1(nc.sbuf_tensor("p1qs", [128, NSLOT * HF], F32))
                kc_ = e1(nc.sbuf_tensor("p1kc", [128, NT128_KV * HF], F32))
                ks_ = e1(nc.sbuf_tensor("p1ks", [128, NT128_KV * HF], F32))
                tlo = e1(nc.sbuf_tensor("p1tlo", [128, NT128_KV * HF], F32))
                tmp = [e1(nc.sbuf_tensor(f"p1tmp{i}", [128, 256], F32)) for i in range(4)]
                evb = [e1(nc.sbuf_tensor(f"p1ev{i}", [128, 2048], BF16)) for i in range(2)]
                p1ps = contextlib.ExitStack()
                psP1 = [p1ps.enter_context(nc.psum_tensor(f"p1ps{i}", [128, 512], F32))
                        for i in range(8)]
                sW = mksem("p1w")
                sLnT = mksem("p1lnt")
                sSt = [mksem("p1s0"), mksem("p1s1")]

                v_tab = 0
                for wk, hi, lo in [(qc, qcos_hi, qcos_lo), (qs_, qsin_hi, qsin_lo),
                                   (kc_, kcos_hi, kcos_lo), (ks_, ksin_hi, ksin_lo)]:
                    n = wk.shape[1]
                    if v_tab:
                        sc.wait("sync", esv, v_tab)
                    sc.dma("sync", sTab, wk[:], hi[:])
                    sc.dma("sync", sTab, tlo[:, 0:n], lo[:])
                    sc.waitF("vector", sTab)
                    v_tab = sc.op("vector", lambda E, wk=wk, n=n: E.tensor_tensor(
                        out=wk[:], in0=wk[:], in1=tlo[:, 0:n], op=ALU.add), esv)

                def rope_evac(bank, evslice, ctab, stab, tile_idx, v_mm_done, v_dep):
                    cP = ctab[:, ts(tile_idx, HF)].rearrange(
                        "p (o j) -> p o j", o=1).broadcast_to((128, 4, HF))
                    sP = stab[:, ts(tile_idx, HF)].rearrange(
                        "p (o j) -> p o j", o=1).broadcast_to((128, 4, HF))
                    bk = bank.ap().rearrange("p (h l j) -> p h l j", h=4, l=2)
                    ev = evslice.rearrange("p (h l j) -> p h l j", h=4, l=2)
                    top, bot = bk[:, :, 0, :], bk[:, :, 1, :]
                    tv = [t.ap().rearrange("p (h j) -> p h j", h=4) for t in tmp]
                    sc.wait("vector", est, v_mm_done)
                    sc.wait("vector", esv, v_dep)
                    sc.op("vector", lambda E: E.tensor_tensor(out=tv[0], in0=top, in1=cP, op=ALU.mult))
                    sc.op("vector", lambda E: E.tensor_tensor(out=tv[1], in0=bot, in1=sP, op=ALU.mult))
                    sc.op("vector", lambda E: E.tensor_tensor(out=tv[2], in0=top, in1=sP, op=ALU.mult))
                    v4 = sc.op("vector", lambda E: E.tensor_tensor(out=tv[3], in0=bot, in1=cP, op=ALU.mult), esv)
                    sc.wait("vector", esv, v4)
                    sc.op("vector", lambda E: E.tensor_tensor(
                        out=ev[:, :, 0, :], in0=tv[0], in1=tv[1], op=ALU.subtract), esv)
                    return sc.op("vector", lambda E: E.tensor_tensor(
                        out=ev[:, :, 1, :], in0=tv[2], in1=tv[3], op=ALU.add), esv)

                v_rope_prev = 0
                passes = [
                    ("q", wqT, "tok", qtok, (qc, qs_), TOWN, ln1ownT),
                    ("k", wkT, "tok", ktok, (kc_, ks_), T, ln1kvT),
                    ("v", wvT, "planes", vsc, None, T, ln1kvT),
                ]
                for pname, wsrc, tmode, target, rtabs, ptoks, lnsrc in passes:
                    sc.wait("gpsimd", est, est.n)
                    for half in range(2):
                        sc.dma("gpsimd", sW,
                               W_sb.ap().rearrange("p (c f) -> p c f", c=EC)[:, :, ts(half, 1024)],
                               wsrc.rearrange("(c p) f -> p c f", p=128)[:, :, ts(half, 1024)])
                    vW = sW.n
                    ntb = ptoks // 2048
                    ev_hist = []
                    for tb in range(ntb):
                        sc.wait("sync", est, est.n)
                        for ec in range(EC):
                            sc.dma("sync", sLnT, lnT[:, ts(ec, 2048)],
                                   lnsrc[ec, :, ts(tb, 2048)])
                        sc.waitF("tensor", sLnT)
                        if tb == 0:
                            sc.wait("tensor", sW, vW)
                            sc.wait("tensor", esv, esv.n)
                        for t128 in range(16):
                            grp = t128 % 2
                            tile_global = tb * 16 + t128
                            if len(ev_hist) >= 2:
                                sc.wait("tensor", esv, ev_hist[-2])
                            v_mm = 0
                            for ec in range(EC):
                                lhs = lnT[:, ec * 2048 + t128 * 128: ec * 2048 + (t128 + 1) * 128]
                                for f in range(4):
                                    v_mm = sc.op("tensor", lambda E, grp=grp, f=f, lhs=lhs, ec=ec: E.matmul(
                                        psP1[grp * 4 + f][:], lhs,
                                        W_sb[:, ec * 2048 + f * 512: ec * 2048 + (f + 1) * 512],
                                        start=(ec == 0), stop=(ec == EC - 1)),
                                        est if ec == EC - 1 else None)
                            eb = evb[t128 % 2]
                            sc.waitF("vector", sSt[t128 % 2])
                            v_ev = 0
                            if rtabs is not None:
                                for f in range(4):
                                    v_ev = rope_evac(psP1[grp * 4 + f], eb[:, ts(f, 512)],
                                                     rtabs[0], rtabs[1], tile_global, v_mm,
                                                     max(v_tab, v_rope_prev))
                                    v_rope_prev = v_ev
                            else:
                                sc.wait("vector", est, v_mm)
                                for f in range(4):
                                    v_ev = sc.op("vector", lambda E, f=f, grp=grp, eb=eb: E.tensor_copy(
                                        eb[:, ts(f, 512)], psP1[grp * 4 + f][:]), esv)
                            sc.wait("gpsimd", esv, v_ev)
                            if tmode == "tok":
                                sc.dma("gpsimd", sSt[t128 % 2],
                                       target[ts(tile_global, 128), :], eb[:])
                            else:
                                for h in range(H):
                                    sc.dma("gpsimd", sSt[t128 % 2],
                                           target[h, ts(tile_global, 128), :],
                                           eb[:, ts(h, 128)])
                            ev_hist.append(v_ev)
                sc.waitF("gpsimd", sSt[0])
                sc.waitF("gpsimd", sSt[1])

                p1ps.close()
                sc.waitF("sync", sSt[0])
                sc.waitF("sync", sSt[1])
                psT = [e1(nc.psum_tensor(f"p1psT{i}", [128, 512], BF16)) for i in range(2)]
                s15 = [mksem("p15a"), mksem("p15b")]
                s15s = [mksem("p15s0"), mksem("p15s1")]
                evs15 = []
                trl = {}
                jobs = [(qtok, qTp, i) for i in range(NT128_OWN)] + \
                       [(ktok, kTp, i) for i in range(NT128_KV)]
                for idx, (src, dst, i) in enumerate(jobs):
                    bi = idx % 2
                    if idx >= 2:
                        sc.wait("sync", est, trl[idx - 2])
                    sc.dma("sync", s15[bi], evb[bi][:], src[ts(i, 128), :])
                    sc.waitF("tensor", s15[bi])
                    for q4 in range(4):
                        bk = psT[q4 % 2]
                        if len(evs15) >= 2:
                            sc.wait("tensor", esv, evs15[-2])
                        v_tr = 0
                        for j in range(4):
                            h = q4 * 4 + j
                            v_tr = sc.op("tensor", lambda E, h=h, bk=bk, bi=bi, j=j: E.transpose(
                                bk[:, ts(j, 128)], evb[bi][:, ts(h, 128)], ident_sb[:]), est)
                        if q4 == 3:
                            trl[idx] = v_tr
                        sc.wait("vector", est, v_tr)
                        if q4 == 0:
                            sc.waitF("vector", s15s[bi])
                        evs15.append(sc.op("vector", lambda E, q4=q4, bk=bk, bi=bi: E.tensor_copy(
                            lnT[:, bi * 8192 + q4 * 512: bi * 8192 + (q4 + 1) * 512], bk[:]), esv))
                    sc.wait("gpsimd", esv, evs15[-1])
                    for h in range(H):
                        sc.dma("gpsimd", s15s[bi], dst[h, :, ts(i, 128)],
                               lnT[:, bi * 8192 + h * 128: bi * 8192 + (h + 1) * 128])
                sc.waitF("gpsimd", s15s[0])
                sc.waitF("gpsimd", s15s[1])
                sc.barrier(bar)
        if nphases > 1:
            _phase1()

        def _phase2():
            with contextlib.ExitStack() as p2:
                e2 = p2.enter_context
                QT = e2(nc.sbuf_tensor("p2QT", [128, H * 512], BF16))
                KT = [e2(nc.sbuf_tensor(f"p2KT{i}", [128, T], BF16)) for i in range(2)]
                V = [e2(nc.sbuf_tensor(f"p2V{i}", [128, 32 * 128], BF16)) for i in range(2)]
                PT = [e2(nc.sbuf_tensor(f"p2PT{i}", [128, 512], BF16)) for i in range(4)]
                MK = e2(nc.sbuf_tensor("p2MK", [128, 8 * 512], BF16))
                YS = [e2(nc.sbuf_tensor(f"p2YS{i}", [128, 512], BF16)) for i in range(2)]
                SR = e2(nc.sbuf_tensor("p2SR", [1, 512], F32))
                SR2 = e2(nc.sbuf_tensor("p2SR2", [1, 512], F32))
                BC = e2(nc.sbuf_tensor("p2BC", [128, 512], F32))
                psS = [e2(nc.psum_tensor(f"p2psS{i}", [128, 512], F32)) for i in range(4)]
                psY = [e2(nc.psum_tensor(f"p2psY{i}", [128, 512], F32)) for i in range(2)]
                psSum = e2(nc.psum_tensor("p2psSum", [1, 512], F32))
                psBC = e2(nc.psum_tensor("p2psBC", [128, 512], F32))
                sQT = mksem("p2qt")
                sKV = [mksem("p2kv0"), mksem("p2kv1")]
                sYSt = [mksem("p2y0"), mksem("p2y1")]

                sc.waitF("scalar", sTab)
                sc.waitF("tensor", sTab)
                for g in range(NG):
                    Lg = 1024 * (g + 1)
                    KCg = 8 * (g + 1)
                    sc.dma("sync", sQT, MK[:], pmask[:, g * 8 * 512:(g + 1) * 8 * 512])
                    for h in range(H):
                        sc.dma("sync", sQT, QT[:, ts(h, 512)],
                               qTp[h, :, ts(g, 512)])
                    vQT = sQT.n
                    sc.wait("vector", sQT, vQT)
                    av_last = {}
                    y_vals = {}
                    for h in range(H):
                        hb = h % 2
                        if h >= 2:
                            sc.wait("scalar", est, av_last[h - 2])
                        sc.dma("scalar", sKV[hb], KT[hb][:, 0:Lg],
                               kTp[h, :, 0:Lg])
                        sc.dma("scalar", sKV[hb],
                               V[hb].ap().rearrange("p (c d) -> p c d", d=128)[:, 0:KCg, :],
                               vsc[h, 0:Lg, :].rearrange("(c p) d -> p c d", p=128))
                        if h == 0:
                            sc.wait("tensor", sQT, vQT)
                        sc.waitF("tensor", sKV[hb])
                        if h >= 2:
                            sc.wait("tensor", esv, y_vals[h - 2])
                        v_es = {}
                        av_done = {}
                        pend = []

                        def flush_av(kc, pb, dep_sem, dep_val):
                            sc.wait("tensor", dep_sem, dep_val)
                            sc.op("tensor", lambda E, pb=pb, kc=kc, hb=hb, KCg=KCg: E.matmul(
                                psY[hb][:], V[hb][:, ts(kc, 128)], pb[:],
                                start=(kc == 0), stop=(kc == KCg - 1)), est)
                            av_done[kc] = sc.op("tensor", lambda E, pb=pb, kc=kc, KCg=KCg: E.matmul(
                                psSum[:], onesc_sb[:], pb[:],
                                start=(kc == 0), stop=(kc == KCg - 1)), est)

                        for kc in range(KCg):
                            sb = psS[kc % 4]
                            pb = PT[kc % 4]
                            if kc >= 4:
                                sc.wait("tensor", ess, v_es[kc - 4])
                            v_s = sc.op("tensor", lambda E, sb=sb, kc=kc, hb=hb, h=h: E.matmul(
                                sb[:], KT[hb][:, ts(kc, 128)], QT[:, ts(h, 512)],
                                start=True, stop=True), est)
                            sc.wait("scalar", est, v_s)
                            if kc >= 4:
                                sc.wait("scalar", est, av_done[kc - 4])
                            v_e = sc.op("scalar", lambda E, pb=pb, sb=sb: E.activation(
                                out=pb[:], in_=sb[:], func=AF.Exp, bias=negc0_sb[:]), ess)
                            v_es[kc] = v_e
                            dep_sem, dep_val = ess, v_e
                            if kc >= 8 * g:
                                sc.wait("vector", ess, v_e)
                                v_m = sc.op("vector", lambda E, pb=pb, kc=kc, g=g: E.tensor_tensor(
                                    out=pb[:], in0=pb[:], in1=MK[:, ts(kc - 8 * g, 512)],
                                    op=ALU.mult), esv)
                                dep_sem, dep_val = esv, v_m
                            pend.append((kc, pb, dep_sem, dep_val))
                            if len(pend) > 1:
                                flush_av(*pend.pop(0))
                        while pend:
                            flush_av(*pend.pop(0))
                        av_last[h] = av_done[KCg - 1]
                        sc.wait("vector", est, av_last[h])
                        v_cp = sc.op("vector", lambda E: E.tensor_copy(SR[:], psSum[:]), esv)
                        sc.wait("vector", esv, v_cp)
                        v_sr = sc.op("vector", lambda E: E.reciprocal(SR2[:], SR[:]), esv)
                        sc.wait("tensor", esv, v_sr)
                        v_bc = sc.op("tensor", lambda E: E.matmul(
                            psBC[:], onesr_sb[:], SR2[:], start=True, stop=True), est)
                        sc.wait("vector", est, v_bc)
                        v_bcc = sc.op("vector", lambda E: E.tensor_copy(BC[:], psBC[:]), esv)
                        sc.waitF("vector", sYSt[hb])
                        sc.wait("vector", esv, v_bcc)
                        v_y = sc.op("vector", lambda E, hb=hb: E.tensor_tensor(
                            out=YS[hb][:], in0=psY[hb][:], in1=BC[:], op=ALU.mult), esv)
                        y_vals[h] = v_y
                        sc.wait("gpsimd", esv, v_y)
                        sc.dma("gpsimd", sYSt[hb], ysc[h, :, ts(g, 512)], YS[hb][:])
                    sc.wait("sync", est, est.n)
                sc.waitF("gpsimd", sYSt[0])
                sc.waitF("gpsimd", sYSt[1])
                sc.barrier(bar)
        if nphases > 2:
            _phase2()

        def _phase3():
            with contextlib.ExitStack() as p3:
                e3 = p3.enter_context
                Y = e3(nc.sbuf_tensor("p3Y", [128, H * 2048], BF16))
                WP = [e3(nc.sbuf_tensor(f"p3WP{i}", [128, EC * 128], BF16)) for i in range(2)]
                AT = e3(nc.sbuf_tensor("p3AT", [128, EC * 2048], BF16))
                p3ps = contextlib.ExitStack()
                psP3 = [p3ps.enter_context(nc.psum_tensor(f"p3ps{i}", [128, 512], F32))
                        for i in range(8)]
                sY = mksem("p3y")
                sWP = [mksem("p3w0"), mksem("p3w1")]

                sc.waitF("tensor", sTab)
                for h in range(H):
                    sc.dma("sync", sY, Y[:, ts(h, 2048)], ysc[h, :, :])
                vY = sY.n
                ev_hist = []
                mm_hist = []
                for eo in range(EC):
                    bi = eo % 2
                    if eo >= 2:
                        sc.wait("gpsimd", est, mm_hist[eo - 2])
                    sc.dma("gpsimd", sWP[bi],
                           WP[bi].ap().rearrange("p (c d) -> p c d", c=EC),
                           wprojC[eo].rearrange("(c p) d -> p c d", p=128))
                    sc.waitF("tensor", sWP[bi])
                    if eo == 0:
                        sc.wait("tensor", sY, vY)
                    grp = eo % 2
                    if len(ev_hist) >= 2:
                        sc.wait("tensor", esv, ev_hist[-2])
                    v_mm = 0
                    for h in range(H):
                        wl = WP[bi].ap().rearrange("p (c d) -> p c d", c=EC)[:, h, :]
                        for t in range(4):
                            v_mm = sc.op("tensor", lambda E, grp=grp, t=t, h=h, wl=wl: E.matmul(
                                psP3[grp * 4 + t][:], wl,
                                Y[:, h * 2048 + t * 512: h * 2048 + (t + 1) * 512],
                                start=(h == 0), stop=(h == H - 1)),
                                est if h == H - 1 else None)
                    mm_hist.append(v_mm)
                    sc.wait("vector", est, v_mm)
                    v_ev = 0
                    for t in range(4):
                        v_ev = sc.op("vector", lambda E, grp=grp, t=t, eo=eo: E.tensor_copy(
                            AT[:, eo * 2048 + t * 512: eo * 2048 + (t + 1) * 512],
                            psP3[grp * 4 + t][:]), esv)
                    ev_hist.append(v_ev)
                v_at_done = ev_hist[-1]

                xb = e3(nc.sbuf_tensor("p3xb", [128, E], F32))
                x1b = e3(nc.sbuf_tensor("p3x1", [128, E], F32))
                atok = e3(nc.sbuf_tensor("p3atok", [128, E], BF16))
                ln2b = e3(nc.sbuf_tensor("p3ln2", [128, E], BF16))
                l2buf = e3(nc.sbuf_tensor("p3l2buf", [128, E], BF16))
                stats = mkstats(e3, "p3")
                p3ps.close()
                psT = [e3(nc.psum_tensor(f"p3psT{i}", [128, 512], BF16)) for i in range(2)]
                sX3 = mksem("p3x")
                sX1St = mksem("p3x1st")
                sL2St = mksem("p3l2st")

                sc.wait("tensor", esv, v_at_done)
                prev = {}
                for i in range(NT128_OWN):
                    if i >= 1:
                        sc.wait("sync", esv, prev["add"])
                    sc.dma("sync", sX3, xb[:], x_own[ts(i, 128), :])
                    tr_ev = {}
                    for qtr in range(4):
                        bk = psT[qtr % 2]
                        if qtr >= 2:
                            sc.wait("tensor", esv, tr_ev[qtr - 2])
                        if i >= 1 and qtr < 2:
                            sc.wait("tensor", esv, prev[("trev", qtr + 2)])
                        v_tr = 0
                        for j in range(4):
                            eo = qtr * 4 + j
                            v_tr = sc.op("tensor", lambda E, eo=eo, bk=bk, i=i, j=j: E.transpose(
                                bk[:, ts(j, 128)],
                                AT[:, eo * 2048 + i * 128: eo * 2048 + (i + 1) * 128],
                                ident_sb[:]), est)
                        sc.wait("vector", est, v_tr)
                        if qtr == 0 and i >= 1:
                            sc.wait("vector", esv, prev["add"])
                        tr_ev[qtr] = sc.op("vector", lambda E, qtr=qtr, bk=bk: E.tensor_copy(
                            atok[:, ts(qtr, 512)], bk[:]), esv)
                    sc.waitF("vector", sX3)
                    if i >= 1:
                        sc.wait("vector", ess, prev["ln2"])
                        sc.waitF("vector", sX1St)
                    sc.wait("vector", esv, tr_ev[3])
                    v_add = sc.op("vector", lambda E: E.tensor_tensor(
                        out=x1b[:], in0=xb[:], in1=atok[:], op=ALU.add), esv)
                    v_ln2 = ln_tile(x1b, ln2b, stats, esv, v_add)
                    sc.wait("gpsimd", esv, v_add)
                    sc.dma("gpsimd", sX1St, x1sc[ts(i, 128), :], x1b[:])
                    sc.wait("tensor", ess, v_ln2)
                    l2evs = []
                    for q4 in range(4):
                        bk = psT[q4 % 2]
                        sc.wait("tensor", esv, tr_ev[q4 % 2 + 2] if q4 < 2 else l2evs[q4 - 2])
                        v_tr = 0
                        for j in range(4):
                            ec = q4 * 4 + j
                            v_tr = sc.op("tensor", lambda E, ec=ec, bk=bk, j=j: E.transpose(
                                bk[:, ts(j, 128)], ln2b[:, ts(ec, 128)], ident_sb[:]), est)
                        if q4 == 3:
                            v_l2tr = v_tr
                        sc.wait("vector", est, v_tr)
                        if q4 == 0:
                            sc.waitF("vector", sL2St)
                        l2evs.append(sc.op("vector", lambda E, q4=q4, bk=bk: E.tensor_copy(
                            l2buf[:, ts(q4, 512)], bk[:]), esv))
                    sc.wait("gpsimd", esv, l2evs[-1])
                    for ec in range(EC):
                        sc.dma("gpsimd", sL2St, ln2T[ec, :, ts(i, 128)],
                               l2buf[:, ts(ec, 128)])
                    sc.wait("scalar", est, v_l2tr)
                    prev = {"add": v_add, "ln2": v_ln2,
                            ("trev", 2): l2evs[2], ("trev", 3): l2evs[3]}
                sc.waitF("gpsimd", sX1St)
                sc.waitF("gpsimd", sL2St)
                sc.barrier(bar)
        if nphases > 3:
            _phase3()

        def _phase4():
            with contextlib.ExitStack() as p4:
                e4 = p4.enter_context
                LNT = e4(nc.sbuf_tensor("p4lnT", [128, EC * 2048], BF16))
                GW = [e4(nc.sbuf_tensor(f"p4gw{i}", [128, EC * 128], BF16)) for i in range(2)]
                UW = [e4(nc.sbuf_tensor(f"p4uw{i}", [128, EC * 128], BF16)) for i in range(2)]
                SIL = [e4(nc.sbuf_tensor(f"p4sil{i}", [128, 512], F32)) for i in range(2)]
                HB = e4(nc.sbuf_tensor("p4hb", [128, 2048], BF16))
                psP4 = [e4(nc.psum_tensor(f"p4ps{i}", [128, 512], F32)) for i in range(8)]
                sLnT = mksem("p4lnt")
                sGW = [mksem("p4g0"), mksem("p4g1")]
                sHSt = mksem("p4h")

                for ec in range(EC):
                    sc.dma("sync", sLnT, LNT[:, ts(ec, 2048)], ln2T[ec, :, :])
                vLn = sLnT.n
                sc.waitF("scalar", sTab)
                sc.waitF("vector", sTab)
                hist = []
                sil_hist = []
                for ic in range(IC):
                    bi = ic % 2
                    if ic >= 2:
                        sc.wait("gpsimd", est, hist[ic - 2][0])
                    sc.dma("gpsimd", sGW[bi], GW[bi].ap().rearrange("p (c d) -> p c d", c=EC),
                           gateC[ic].rearrange("(c p) d -> p c d", p=128))
                    sc.dma("gpsimd", sGW[bi], UW[bi].ap().rearrange("p (c d) -> p c d", c=EC),
                           upC[ic].rearrange("(c p) d -> p c d", p=128))
                    sc.waitF("tensor", sGW[bi])
                    if ic == 0:
                        sc.wait("tensor", sLnT, vLn)
                    if ic >= 1:
                        sc.wait("tensor", esv, hist[ic - 1][1])
                    v_mm = 0
                    for ec in range(EC):
                        gl = GW[bi].ap().rearrange("p (c d) -> p c d", c=EC)[:, ec, :]
                        ul = UW[bi].ap().rearrange("p (c d) -> p c d", c=EC)[:, ec, :]
                        for t in range(4):
                            v_mm = sc.op("tensor", lambda E, gl=gl, ec=ec, t=t: E.matmul(
                                psP4[t][:], gl,
                                LNT[:, ec * 2048 + t * 512: ec * 2048 + (t + 1) * 512],
                                start=(ec == 0), stop=(ec == EC - 1)),
                                est if ec == EC - 1 else None)
                        for t in range(4):
                            v_mm = sc.op("tensor", lambda E, ul=ul, ec=ec, t=t: E.matmul(
                                psP4[4 + t][:], ul,
                                LNT[:, ec * 2048 + t * 512: ec * 2048 + (t + 1) * 512],
                                start=(ec == 0), stop=(ec == EC - 1)),
                                est if ec == EC - 1 else None)
                    if ic >= 1:
                        sc.waitF("vector", sHSt)
                    v_ev = 0
                    for t in range(4):
                        sb_sil = SIL[t % 2]
                        sc.wait("scalar", est, v_mm)
                        if len(sil_hist) >= 2:
                            sc.wait("scalar", esv, sil_hist[-2])
                        v_sil = sc.op("scalar", lambda E, t=t, ic=ic, sb_sil=sb_sil: E.activation(
                            out=sb_sil[:], in_=psP4[t][:], func=AF.Silu,
                            scale=gs_sb[:, ic:ic + 1]), ess)
                        sc.wait("vector", ess, v_sil)
                        v_ev = sc.op("vector", lambda E, t=t, ic=ic, sb_sil=sb_sil: E.scalar_tensor_tensor(
                            out=HB[:, ts(t, 512)], in0=psP4[4 + t][:],
                            scalar=us_sb[:, ic:ic + 1], in1=sb_sil[:],
                            op0=ALU.mult, op1=ALU.mult), esv)
                        sil_hist.append(v_ev)
                    sc.wait("gpsimd", esv, v_ev)
                    sc.dma("gpsimd", sHSt, hsc[ic, :, :], HB[:])
                    hist.append((v_mm, v_ev))
                sc.waitF("gpsimd", sHSt)
                sc.barrier(bar)
        if nphases > 4:
            _phase4()

        def _phase5():
            with contextlib.ExitStack() as p5:
                e5 = p5.enter_context
                HH = e5(nc.sbuf_tensor("p5H", [128, IC * 1024], BF16))
                DW = [e5(nc.sbuf_tensor(f"p5dw{i}", [128, IC * 128], BF16)) for i in range(2)]
                FO = e5(nc.sbuf_tensor("p5fo", [128, EC * 1024], BF16))
                x1b5 = e5(nc.sbuf_tensor("p5x1", [128, E], F32))
                ftok = e5(nc.sbuf_tensor("p5ftok", [128, E], BF16))
                ob = e5(nc.sbuf_tensor("p5ob", [128, E], F32))
                psP5 = [e5(nc.psum_tensor(f"p5ps{i}", [128, 512], F32)) for i in range(4)]
                psT = [e5(nc.psum_tensor(f"p5psT{i}", [128, 512], BF16)) for i in range(2)]
                sH = mksem("p5h")
                sDW = [mksem("p5w0"), mksem("p5w1")]
                sX15 = mksem("p5x1")
                sOut = mksem("p5out")

                sc.waitF("scalar", sTab)
                for half in range(2):
                    sc.wait("sync", est, est.n)
                    sc.wait("gpsimd", est, est.n)
                    sc.wait("tensor", ess, ess.n)
                    sc.wait("tensor", esv, esv.n)
                    for ic in range(IC):
                        sc.dma("sync", sH, HH[:, ts(ic, 1024)],
                               hsc[ic, :, ts(half, 1024)])
                    vH = sH.n
                    ev_hist = []
                    mm_hist = []
                    for eo in range(EC):
                        bi = eo % 2
                        if eo >= 2:
                            sc.wait("gpsimd", est, mm_hist[eo - 2])
                        sc.dma("gpsimd", sDW[bi],
                               DW[bi].ap().rearrange("p (c d) -> p c d", c=IC),
                               downC[eo].rearrange("(c p) d -> p c d", p=128))
                        sc.waitF("tensor", sDW[bi])
                        if eo == 0:
                            sc.wait("tensor", sH, vH)
                        if len(ev_hist) >= 2:
                            sc.wait("tensor", ess, ev_hist[-2])
                        grp = eo % 2
                        v_mm = 0
                        for ic in range(IC):
                            dl = DW[bi].ap().rearrange("p (c d) -> p c d", c=IC)[:, ic, :]
                            for t in range(2):
                                v_mm = sc.op("tensor", lambda E, dl=dl, ic=ic, t=t, grp=grp: E.matmul(
                                    psP5[grp * 2 + t][:], dl,
                                    HH[:, ic * 1024 + t * 512: ic * 1024 + (t + 1) * 512],
                                    start=(ic == 0), stop=(ic == IC - 1)),
                                    est if ic == IC - 1 else None)
                        mm_hist.append(v_mm)
                        sc.wait("scalar", est, v_mm)
                        v_ev = 0
                        for t in range(2):
                            v_ev = sc.op("scalar", lambda E, eo=eo, t=t, grp=grp: E.activation(
                                out=FO[:, eo * 1024 + t * 512: eo * 1024 + (t + 1) * 512],
                                in_=psP5[grp * 2 + t][:], func=AF.Copy,
                                scale=ds_sb[:, eo:eo + 1]), ess)
                        ev_hist.append(v_ev)
                    v_fo = ev_hist[-1]
                    sc.wait("tensor", ess, v_fo)
                    if half == 0:
                        prev = {}
                    for ii in range(8):
                        i = half * 8 + ii
                        if prev.get("o"):
                            sc.wait("sync", esv, prev["o"])
                        sc.dma("sync", sX15, x1b5[:], x1sc[ts(i, 128), :])
                        tr_ev = {}
                        for qtr in range(4):
                            bk = psT[qtr % 2]
                            if qtr >= 2:
                                sc.wait("tensor", esv, tr_ev[qtr - 2])
                            if qtr < 2 and prev.get(("trev", qtr + 2)):
                                sc.wait("tensor", esv, prev[("trev", qtr + 2)])
                            v_tr = 0
                            for j in range(4):
                                eo = qtr * 4 + j
                                v_tr = sc.op("tensor", lambda E, eo=eo, bk=bk, ii=ii, j=j: E.transpose(
                                    bk[:, ts(j, 128)],
                                    FO[:, eo * 1024 + ii * 128: eo * 1024 + (ii + 1) * 128],
                                    ident_sb[:]), est)
                            sc.wait("vector", est, v_tr)
                            if qtr == 0 and prev.get("o"):
                                sc.wait("vector", esv, prev["o"])
                            tr_ev[qtr] = sc.op("vector", lambda E, qtr=qtr, bk=bk: E.tensor_copy(
                                ftok[:, ts(qtr, 512)], bk[:]), esv)
                        sc.waitF("vector", sX15)
                        sc.waitF("vector", sOut)
                        sc.wait("vector", esv, tr_ev[3])
                        v_o = sc.op("vector", lambda E: E.tensor_tensor(
                            out=ob[:], in0=x1b5[:], in1=ftok[:], op=ALU.add), esv)
                        sc.wait("sync", esv, v_o)
                        sc.dma("sync", sOut, out[ts(i, 128), :], ob[:])
                        prev = {"o": v_o, ("trev", 2): tr_ev[2], ("trev", 3): tr_ev[3]}
                sc.waitF("sync", sOut)
        if nphases > 5:
            _phase5()

        block = ecm(nc.Block())

        @block.tensor
        def _(Eng):
            for f in sc.q["tensor"]:
                f(Eng)

        @block.vector
        def _(Eng):
            for f in sc.q["vector"]:
                f(Eng)

        @block.scalar
        def _(Eng):
            for f in sc.q["scalar"]:
                f(Eng)

        @block.sync
        def _(Eng):
            for f in sc.q["sync"]:
                f(Eng)

        @block.gpsimd
        def _(Eng):
            for f in sc.q["gpsimd"]:
                f(Eng)

    return nc


def _host_tables(parity):
    inv = 1.0 / (10000.0 ** (np.arange(0, D, 2, dtype=np.float64) / D))

    def tab(tvals, scale):
        ang = tvals.reshape(-1, 1) * inv[None, :]
        return np.cos(ang) * scale, np.sin(ang) * scale

    tk = (np.arange(NT128_KV)[None, :] * 128 + np.arange(128)[:, None]).astype(np.float64)
    ck, sk = tab(tk.reshape(-1), 1.0)
    ck = ck.reshape(128, NT128_KV, HF).reshape(128, -1)
    sk = sk.reshape(128, NT128_KV, HF).reshape(128, -1)
    tq = (128 * (2 * np.arange(NSLOT)[None, :] + parity) + np.arange(128)[:, None]).astype(np.float64)
    cq, sq = tab(tq.reshape(-1), 1.0 / np.sqrt(np.float64(D)))
    cq = cq.reshape(128, NSLOT, HF).reshape(128, -1)
    sq = sq.reshape(128, NSLOT, HF).reshape(128, -1)

    def hilo(a):
        hi = a.astype(np.float32)
        lo = (a - hi.astype(np.float64)).astype(np.float32)
        return np.ascontiguousarray(hi), np.ascontiguousarray(lo)
    return [hilo(x) for x in (cq, sq, ck, sk)]


def _host_masks(parity):
    m = np.zeros((128, NG, 8, 512), dtype=np.float32)
    kk = np.arange(128)[:, None]
    for g in range(NG):
        for kc in range(8):
            kglob = 1024 * g + 128 * kc + kk
            sl = 4 * g + np.arange(512)[None, :] // 128
            qglob = 256 * sl + 128 * parity + (np.arange(512)[None, :] % 128)
            m[:, g, kc, :] = (kglob <= qglob).astype(np.float32)
    return np.ascontiguousarray(m.reshape(128, -1)).astype(BF)


def _qk_perm():
    p = np.zeros(D, dtype=np.int64)
    for l in range(2):
        for j in range(HF):
            p[l * HF + j] = 2 * j + l
    return p


def _host_weights(Wqkv, Wproj, gw, uw, dw):
    blk = _qk_perm()
    perm_h = np.concatenate([h * D + blk for h in range(H)])
    WT = Wqkv.T
    wqT = np.ascontiguousarray(WT[:, 0:E][:, perm_h])
    wkT = np.ascontiguousarray(WT[:, E:2 * E][:, perm_h])
    wvT = np.ascontiguousarray(WT[:, 2 * E:3 * E])
    wprojC = np.ascontiguousarray(Wproj.T.reshape(E, EC, 128).transpose(1, 0, 2))
    gateC = np.ascontiguousarray(gw.T.reshape(E, IC, 128).transpose(1, 0, 2))
    upC = np.ascontiguousarray(uw.T.reshape(E, IC, 128).transpose(1, 0, 2))
    downC = np.ascontiguousarray(dw.T.reshape(I, EC, 128).transpose(1, 0, 2))
    return wqT, wkT, wvT, wprojC, gateC, upC, downC


def _host_inmap(core, x, wqT, wkT, wvT, wprojC, gateC, upC, downC,
                gs, us, ds, tables, masks, own_rows):
    b, p = core // 2, core % 2
    (cq_h, cq_l), (sq_h, sq_l), (ck_h, ck_l), (sk_h, sk_l) = tables[p]
    return {
        "x_kv": x[b], "x_own": np.ascontiguousarray(x[b][own_rows[p]]),
        "wqT": wqT, "wkT": wkT, "wvT": wvT,
        "wprojC": wprojC, "gateC": gateC, "upC": upC, "downC": downC,
        "gate_s": np.ascontiguousarray(gs.reshape(IC, 128).T),
        "up_s": np.ascontiguousarray(us.reshape(IC, 128).T),
        "down_s": np.ascontiguousarray(ds.reshape(EC, 128).T),
        "qcos_hi": cq_h, "qcos_lo": cq_l, "qsin_hi": sq_h, "qsin_lo": sq_l,
        "kcos_hi": ck_h, "kcos_lo": ck_l, "ksin_hi": sk_h, "ksin_lo": sk_l,
        "pmask": masks[p],
        "ident_bf": np.eye(128, dtype=np.float32).astype(BF),
        "ones_col_bf": np.ones((128, 1), np.float32).astype(BF),
        "ones_row_f": np.ones((1, 128), np.float32),
        "negc0": np.full((128, 1), -C0, np.float32),
    }


def kernel(L0_attn_x, L0_attn_W_qkv, L0_attn_W_proj, L0_ffn_gate_w, L0_ffn_up_w,
           L0_ffn_down_w, L0_ffn_gate_s, L0_ffn_up_s, L0_ffn_down_s):
    x = np.ascontiguousarray(np.asarray(L0_attn_x, dtype=np.float32))
    Wqkv = np.asarray(L0_attn_W_qkv, dtype=np.float32)
    Wproj = np.asarray(L0_attn_W_proj, dtype=np.float32)
    gw = np.asarray(L0_ffn_gate_w, dtype=np.float32)
    uw = np.asarray(L0_ffn_up_w, dtype=np.float32)
    dw = np.asarray(L0_ffn_down_w, dtype=np.float32)
    gs = np.asarray(L0_ffn_gate_s, dtype=np.float32)
    us = np.asarray(L0_ffn_up_s, dtype=np.float32)
    ds = np.asarray(L0_ffn_down_s, dtype=np.float32)

    wqT, wkT, wvT, wprojC, gateC, upC, downC = _host_weights(Wqkv, Wproj, gw, uw, dw)
    tables = [_host_tables(p) for p in range(2)]
    masks = [_host_masks(p) for p in range(2)]
    own_rows = [np.concatenate([np.arange(128 * (2 * j + p), 128 * (2 * j + p) + 128)
                                for j in range(NSLOT)]) for p in range(2)]

    nc = build_program()
    in_maps = [_host_inmap(c, x, wqT, wkT, wvT, wprojC, gateC, upC, downC,
                           gs, us, ds, tables, masks, own_rows)
               for c in range(8)]
    res = run_bass_kernel_spmd(nc, in_maps, core_ids=list(range(8)))
    outp = np.empty((B, T, E), dtype=np.float32)
    for c in range(8):
        b, p = c // 2, c % 2
        outp[b][own_rows[p]] = res.results[c]["out"]
    return outp
